# version 04
# baseline (speedup 1.0000x reference)
import dataclasses
from contextlib import ExitStack

import numpy as np
import ml_dtypes
import concourse.bass as bass
import concourse.mybir as mybir
from concourse.bass_utils import run_bass_kernel_spmd
from concourse.alu_op_type import AluOpType as AO

F32 = mybir.dt.float32
BF16 = mybir.dt.bfloat16
FP16 = mybir.dt.float16
AF = mybir.ActivationFunctionType
AX = mybir.AxisListType

T = 2048
E = 2048
NT = 16
NE = 16
H = 16
D = 128
S = 32
R = 16
LN_EPS = 1e-5
SQ_SCALE = 1.0 / 16.0


def _v(ap2, dims, doff=0):
    return dataclasses.replace(
        ap2, ap=[list(ap2.ap[0])] + [list(d) for d in dims],
        offset=ap2.offset + doff)


class Prog:
    def __init__(self):
        self.steps = []
        self.cnt = {}
        self.cp = {}

    def step(self, engine, emit, waits=(), inc=None, inc_amt=1, tag=""):
        ws = [(s, v) for (s, v) in waits if v and v > 0]
        self.steps.append(dict(engine=engine, emit=emit, waits=ws, inc=inc,
                               inc_amt=inc_amt, tag=tag))
        if inc is not None:
            self.cnt[inc] = self.cnt.get(inc, 0) + inc_amt
            return self.cnt[inc]
        return None

    def c(self, key):
        return self.cp.get(key, 0)


def build(cut=None, taps=()):
    nc = bass.Bass("TRN2", target_bir_lowering=False, debug=False)

    xd = nc.dram_tensor("x", [T, E], F32, kind="ExternalInput").ap()
    xTd = nc.dram_tensor("xT", [E, T], F32, kind="ExternalInput").ap()
    wqkvd = nc.dram_tensor("wqkv", [E, 3 * E], F32, kind="ExternalInput").ap()
    woutd = nc.dram_tensor("wout", [E, E], F32, kind="ExternalInput").ap()
    upermd = nc.dram_tensor("uperm", [E, S * R], F32, kind="ExternalInput").ap()
    vflatd = nc.dram_tensor("vflat", [S * R, E], F32, kind="ExternalInput").ap()
    wrtd = nc.dram_tensor("wrt", [E, S], F32, kind="ExternalInput").ap()
    attrTd = nc.dram_tensor("attrT", [S, S], F32, kind="ExternalInput").ap()
    waTd = nc.dram_tensor("waT", [S, 1], F32, kind="ExternalInput").ap()
    cosd = nc.dram_tensor("cosr", [128, NT * 64], F32, kind="ExternalInput").ap()
    sind = nc.dram_tensor("sinr", [128, NT * 64], F32, kind="ExternalInput").ap()
    scbd = nc.dram_tensor("scb", [128, D], F32, kind="ExternalInput").ap()
    bibd = nc.dram_tensor("bib", [128, D], F32, kind="ExternalInput").ap()
    trild = nc.dram_tensor("trilm", [128, 128], F32, kind="ExternalInput").ap()
    identd = nc.dram_tensor("ident", [128, 128], BF16, kind="ExternalInput").ap()
    yd = nc.dram_tensor("y", [T, E], F32, kind="ExternalOutput").ap()
    tap_outs = {}
    for name, _ap, shape, dt in taps:
        tap_outs[name] = nc.dram_tensor("tap_" + name, shape, dt,
                                        kind="ExternalOutput").ap()

    with ExitStack() as es:
        def sbt(name, shape, dt):
            return es.enter_context(nc.sbuf_tensor(name, shape, dt))

        big = sbt("big", [128, 32768], BF16)
        aoT = sbt("aoT", [128, 32768], BF16)
        bfar = sbt("bfar", [128, 28544], BF16)
        far = sbt("far", [128, 4352], F32)
        s2h = sbt("s2h", [128, 1536], FP16)
        tril16 = sbt("tril16", [128, 128], FP16)
        idb = sbt("idb", [128, 128], BF16)
        wrtb = sbt("wrtb", [128, NE * S], BF16)
        awb = sbt("awb", [128, S], F32)
        attrb = sbt("attrb", [S, S], BF16)
        watb = sbt("watb", [S, 1], BF16)
        awf = sbt("awf", [S, 2], F32)
        awbf = sbt("awbf", [S, 1], BF16)
        awrow = sbt("awrow", [1, S], BF16)
        ones1 = sbt("ones1", [1, 128], BF16)
        epsc = sbt("epsc", [128, 1], F32)

        QK2 = 0
        QT, KT = 8192, 10240
        VX2 = 12288
        ATG, AOTM, WQ = 16512, 18048, 20096
        COSB, SINB, SCB, BIB = 26240, 27264, 28288, 28416
        UB, VB, H1B, H1T, WGB, WGT = 0, 8192, 16512, 19072, 21120, 21632
        FB, FC, GAT = 0, 2048, 4096

        pbig = es.enter_context(nc.psum_tensor("pbig", [128, 3072], F32))
        pT = es.enter_context(nc.psum_tensor("pT", [128, 2048], BF16))

        semnames = ["pe", "dv", "ac", "pl", "dxt", "dwq", "dcn", "dwo", "duv",
                    "dxr", "dy", "dtap", "cdv", "cac", "cpl"]
        sems = {n: es.enter_context(nc.semaphore(n)) for n in semnames}
        block = es.enter_context(nc.Block())

        P = Prog()
        ccnt = {"dv": 0, "ac": 0, "pl": 0}

        def chain(e, key, ins):
            ccnt[key] += 1
            ins.then_inc(sems["c" + key], 1)
            e.wait_ge(sems["c" + key], ccnt[key])
            return ins

        def bf(c0, c1):
            return bfar[:, c0:c1]

        stats = far[:, GAT:GAT + 256]
        gat = far[:, GAT:GAT + 256]
        (G_M1, G_NM1, G_M2, G_Z, G_S0, G_DG, G_RG, G_GAW, G_GF,
         G_L1S, G_L1Q, G_NMU1, G_RS1, G_L2S, G_L2Q, G_NMU2, G_RS2,
         G_V1, G_V2, G_TMP) = range(20)
        G_TMP2 = 20
        G_E1, G_MK1, G_SCF, G_SC2, G_MK, G_G0, G_GN = 32, 64, 96, 128, 160, 192, 224

        def gcol(c):
            return gat[:, c:c + 1]

        P.step("sy", lambda e: e.dma_start(out=idb[:, :], in_=identd),
               inc="dcn", inc_amt=16, tag="idb")
        gp_consts = [
            (bf(COSB, COSB + 1024), cosd),
            (bf(SINB, SINB + 1024), sind),
            (bf(SCB, SCB + 128), scbd),
            (bf(BIB, BIB + 128), bibd),
            (tril16[:, :], trild),
            (attrb[:, :], attrTd),
            (watb[:, :], waTd),
            (wrtb[:, :].rearrange("p (ec s) -> p ec s", ec=NE),
             wrtd.rearrange("(ec p) s -> p ec s", p=128)),
        ]
        for (o, i) in gp_consts:
            P.step("gp", (lambda e, o=o, i=i: e.dma_start(out=o, in_=i)),
                   inc="dcn", inc_amt=16, tag="const")
        CN_ALL = P.cnt["dcn"]
        for c in range(NE):
            P.step("gp", (lambda e, c=c: e.dma_start(
                out=big[:, c * 2048:(c + 1) * 2048],
                in_=xTd[c * 128:(c + 1) * 128, :])),
                inc="dxt", inc_amt=16, tag=f"xt{c}")

        def wq_dma_steps(h, waits):
            wqv = bfar[:, WQ:WQ + 6144].rearrange("p (ec c) -> p ec c", c=384)
            for qkv in range(3):
                off = qkv * E + h * 128
                P.step("gp", (lambda e, qkv=qkv, off=off, wqv=wqv: e.dma_start(
                    out=wqv[:, :, qkv * 128:(qkv + 1) * 128],
                    in_=wqkvd[:, off:off + 128].rearrange("(ec p) d -> p ec d", p=128))),
                    waits=waits, inc="dwq", inc_amt=16, tag=f"wq{h}.{qkv}")
        wq_dma_steps(0, ())

        def em_setup(e):
            for par in range(2):
                vxo = _v(bf(VX2 + par * 2112, VX2 + par * 2112 + 2112),
                         [[132, NT], [1, 1]], doff=128)
                e.memset(vxo, 1.0)
            e.memset(epsc[:, :], LN_EPS)
            return e.memset(ones1[:, :], 1.0)
        P.step("dv", em_setup, inc="dv", tag="ones")

        cp_awmm = P.step("te", lambda e: e.matmul(
            pbig[0:S, 0:1], attrb[:, :], watb[:, :], start=True, stop=True),
            waits=[("dcn", CN_ALL)], inc="pe", tag="awmm")
        cp_awexp = P.step("ac", lambda e: e.activation(
            awf[:, 0:1], pbig[0:S, 0:1], AF.Exp, scale=-1.0),
            waits=[("pe", cp_awmm)], inc="ac", tag="awexp")

        def em_awsig(e):
            chain(e, "dv", e.tensor_scalar(awf[:, 1:2], awf[:, 0:1], 1.0, None, op0=AO.add))
            chain(e, "dv", e.reciprocal(awf[:, 0:1], awf[:, 1:2]))
            return e.tensor_copy(awbf[:, :], awf[:, 0:1])
        cp_awsig = P.step("dv", em_awsig, waits=[("ac", cp_awexp)], inc="dv",
                          tag="awsig")
        cp_awtr = P.step("te", lambda e: e.transpose(
            pT[0:1, 0:S], awbf[:, :], idb[0:S, 0:S]),
            waits=[("dv", cp_awsig)], inc="pe", tag="awtr")
        cp_awrow = P.step("dv", lambda e: e.tensor_copy(awrow[:, :], pT[0:1, 0:S]),
                          waits=[("pe", cp_awtr)], inc="dv", tag="awrowcp")
        cp_awbc = P.step("te", lambda e: e.matmul(
            pbig[:, 64:64 + S], ones1[:, :], awrow[:, :], start=True, stop=True),
            waits=[("dv", cp_awrow)], inc="pe", tag="awbc")
        cp_awbcp = P.step("dv", lambda e: e.tensor_copy(awb[:, :], pbig[:, 64:64 + S]),
                          waits=[("pe", cp_awbc)], inc="dv", tag="awbcp")

        gpk = [0]
        g16 = [0]

        def tpack(srcs, dst, waits, tag):
            g = gpk[0]
            slot = pT[:, (g % 2) * 1024:(g % 2) * 1024 + 512]
            w = list(waits)
            pv = P.c(("tpev", g - 2))
            if pv:
                w.append(("dv", pv))

            def em(e, srcs=srcs, slot=slot):
                last = None
                for c, s in enumerate(srcs):
                    last = e.transpose(slot[:, c * 128:(c + 1) * 128], s, idb[:, :])
                return last
            P.cp[("tp", g)] = P.step("te", em, waits=w, inc="pe", tag="tp" + tag)
            P.cp[("tpev", g)] = P.step(
                "dv", (lambda e, slot=slot, dst=dst: e.tensor_copy(dst, slot)),
                waits=[("pe", P.c(("tp", g)))], inc="dv", tag="tpev" + tag)
            gpk[0] = g + 1

        def emit_proj(hh, hooks=()):
            hooks = dict(hooks)
            par = hh % 2
            qkb = QK2 + par * 4096
            vxb = VX2 + par * 2112
            for k in range(NT):
                g = g16[0]
                slot = g % 2
                psv = pbig[:, slot * 512:(slot + 1) * 512]
                w = []
                if k == 0:
                    w.append(("dwq", 48 * (hh + 1)))
                    if hh == 0:
                        w += [("dxt", 256), ("dv", cp_awbcp)]
                ev = P.c(("pevac", g - 2))
                if ev:
                    w.append(("ac", ev))

                def em_p(e, k=k, psv=psv):
                    last = None
                    for ec in range(NE):
                        last = e.matmul(
                            psv[:, 0:384],
                            big[:, ec * 2048 + k * 128:ec * 2048 + k * 128 + 128],
                            bfar[:, WQ + ec * 384:WQ + (ec + 1) * 384],
                            start=(ec == 0), stop=(ec == NE - 1))
                    return last
                P.cp[("p16", hh, k)] = P.step("te", em_p, waits=w, inc="pe",
                                              tag=f"p16.{hh}.{k}")

                def em_ev(e, k=k, psv=psv, qkb=qkb, vxb=vxb):
                    e.copy(bf(qkb + k * 256, qkb + (k + 1) * 256), psv[:, 0:256])
                    return e.copy(bf(vxb + k * 132, vxb + k * 132 + 128),
                                  psv[:, 256:384])
                P.cp[("pevac", g)] = P.step(
                    "ac", em_ev, waits=[("pe", P.c(("p16", hh, k)))],
                    inc="ac", tag=f"pev{hh}.{k}")
                g16[0] = g + 1
                if k in hooks:
                    hooks[k]()
            P.cp[("projlast", hh)] = P.c(("p16", hh, NT - 1))
            P.cp[("pevlast", hh)] = P.c(("pevac", g16[0] - 1))

        def rope_emit(e, base, qkoff, s0, s1, s2c, ck):
            q1 = _v(bf(base, base + 4096), [[256, NT], [1, 64]], doff=qkoff)
            q2 = _v(bf(base, base + 4096), [[256, NT], [1, 64]], doff=qkoff + 64)
            cosv = _v(bf(COSB, COSB + 1024), [[64, NT], [1, 64]])
            sinv = _v(bf(SINB, SINB + 1024), [[64, NT], [1, 64]])
            ua = _v(bf(s0, s0 + 1024), [[64, NT], [1, 64]])
            ua2 = _v(bf(s1, s1 + 1024), [[64, NT], [1, 64]])
            ub = _v(bf(s2c, s2c + 1024), [[64, NT], [1, 64]])
            e.tensor_tensor(ua, q1, cosv, op=AO.mult)
            e.tensor_tensor(ua2, q1, sinv, op=AO.mult)
            chain(e, ck, e.tensor_tensor(ub, q2, sinv, op=AO.mult))
            chain(e, ck, e.tensor_tensor(q1, ua, ub, op=AO.subtract))
            chain(e, ck, e.tensor_tensor(ub, q2, cosv, op=AO.mult))
            return e.tensor_tensor(q2, ua2, ub, op=AO.add)

        scbv = _v(bf(SCB, SCB + 128), [[0, NT], [1, 128]])
        bibv = _v(bf(BIB, BIB + 128), [[0, NT], [1, 128]])
        sqsv = _v(far[:, FB:FB + 2048], [[128, NT], [1, 128]])

        def em_apply(e, base, qkoff, muc, rsc):
            ins = None
            for tt in range(NT):
                sl = bf(base + tt * 256 + qkoff, base + tt * 256 + qkoff + 128)
                ins = e.tensor_scalar(sl, sl,
                                      stats[:, muc + tt:muc + tt + 1],
                                      stats[:, rsc + tt:rsc + tt + 1],
                                      op0=AO.add, op1=AO.mult)
            chain(e, "dv", ins)
            hv = _v(bf(base, base + 4096), [[256, NT], [1, 128]], doff=qkoff)
            chain(e, "dv", e.tensor_tensor(hv, hv, scbv, op=AO.mult))
            return e.tensor_tensor(hv, hv, bibv, op=AO.add)

        def emit_ew_start(h):
            par = h % 2
            base = QK2 + par * 4096
            if h == H - 1:
                sq_ = (WQ, WQ + 1024, WQ + 2048)
                sk_ = (WQ + 3072, WQ + 4096, WQ + 5120)
                wq_ = [("ac", P.c(("pevlast", h))), ("pe", P.c(("projlast", h)))]
                wk_ = list(wq_)
            else:
                sq_ = (QT, QT + 1024, KT)
                sk_ = (AOTM, AOTM + 1024, ATG)
                wq_ = [("ac", P.c(("pevlast", h)))]
                scl = P.c(("scl", h - 1))
                if scl:
                    wq_.append(("pe", scl))
                wk_ = [("ac", P.c(("pevlast", h)))]
                atr = P.c(("aotrl", h - 1))
                if atr:
                    wk_.append(("pe", atr))
            if h == 0:
                wq_.append(("dcn", CN_ALL))
                wk_.append(("dcn", CN_ALL))
            P.cp[("ropeq", h)] = P.step(
                "dv", (lambda e, base=base, s=sq_: rope_emit(e, base, 0, s[0], s[1], s[2], "dv")),
                waits=wq_, inc="dv", tag=f"ropeq{h}")
            P.cp[("ropek", h)] = P.step(
                "gp", (lambda e, base=base, s=sk_: rope_emit(e, base, 128, s[0], s[1], s[2], "pl")),
                waits=wk_, inc="pl", tag=f"ropek{h}")

        def em_red2(e, base, qkoff, c0, c1):
            src = _v(bf(base, base + 4096), [[256, NT], [1, 128]], doff=qkoff)
            e.tensor_reduce(_v(stats[:, c0:c0 + 16], [[1, 16], [1, 1]]),
                            src, axis=AX.X, op=AO.add)
            return e.tensor_reduce(_v(stats[:, c1:c1 + 16], [[1, 16], [1, 1]]),
                                   sqsv, axis=AX.X, op=AO.add)

        def emit_sq(h, which):
            par = h % 2
            base = QK2 + par * 4096
            qkoff = 0 if which == "q" else 128
            src = _v(bf(base, base + 4096), [[256, NT], [1, 128]], doff=qkoff)
            if which == "q":
                w = [("dv", P.c(("ropeq", h)))]
            else:
                w = [("pl", P.c(("ropek", h))), ("dv", P.c(("redq", h)))]
            P.cp[("sq" + which, h)] = P.step(
                "ac", (lambda e, src=src: e.activation(sqsv, src, AF.Square)),
                waits=w, inc="ac", tag=f"sq{which}{h}")

        def emit_redq(h):
            par = h % 2
            base = QK2 + par * 4096
            P.cp[("redq", h)] = P.step(
                "dv", (lambda e, base=base: em_red2(e, base, 0, 0, 16)),
                waits=[("ac", P.c(("sqq", h)))], inc="dv", tag=f"redq{h}")

        def emit_redk_stat(h):
            par = h % 2
            base = QK2 + par * 4096
            P.cp[("redk", h)] = P.step(
                "dv", (lambda e, base=base: em_red2(e, base, 128, 32, 48)),
                waits=[("ac", P.c(("sqk", h)))], inc="dv", tag=f"redk{h}")

            def em_statm(e):
                e.tensor_scalar(stats[:, 64:80], stats[:, 0:16], -1.0 / D, None, op0=AO.mult)
                e.tensor_scalar(stats[:, 96:112], stats[:, 32:48], -1.0 / D, None, op0=AO.mult)
                e.tensor_scalar(stats[:, 128:144], stats[:, 16:32], 1.0 / D, None, op0=AO.mult)
                chain(e, "dv", e.tensor_scalar(stats[:, 144:160], stats[:, 48:64], 1.0 / D, None, op0=AO.mult))
                chain(e, "dv", e.tensor_tensor(stats[:, 160:176], stats[:, 64:80], stats[:, 64:80], op=AO.mult))
                chain(e, "dv", e.tensor_tensor(stats[:, 128:144], stats[:, 128:144], stats[:, 160:176], op=AO.subtract))
                chain(e, "dv", e.tensor_tensor(stats[:, 160:176], stats[:, 96:112], stats[:, 96:112], op=AO.mult))
                return e.tensor_tensor(stats[:, 144:160], stats[:, 144:160], stats[:, 160:176], op=AO.subtract)
            P.cp[("statm", h)] = P.step("dv", em_statm,
                                        waits=[("dv", P.c(("redk", h)))],
                                        inc="dv", tag=f"statm{h}")

        def emit_statact(h):
            def em(e):
                chain(e, "ac", e.activation(stats[:, 160:176], stats[:, 128:144], AF.Ln, bias=epsc[:, 0:1]))
                chain(e, "ac", e.activation(stats[:, 80:96], stats[:, 160:176], AF.Exp, scale=-0.5))
                chain(e, "ac", e.activation(stats[:, 160:176], stats[:, 144:160], AF.Ln, bias=epsc[:, 0:1]))
                return e.activation(stats[:, 112:128], stats[:, 160:176], AF.Exp, scale=-0.5)
            P.cp[("statact", h)] = P.step(
                "ac", em, waits=[("dv", P.c(("statm", h)))], inc="ac",
                tag=f"statact{h}")

        def emit_apply(h):
            par = h % 2
            base = QK2 + par * 4096
            P.cp[("applyq", h)] = P.step(
                "dv", (lambda e, base=base: em_apply(e, base, 0, 64, 80)),
                waits=[("ac", P.c(("statact", h))), ("pl", P.c(("ropek", h)))],
                inc="dv", tag=f"applyq{h}")
            P.cp[("applyk", h)] = P.step(
                "dv", (lambda e, base=base: em_apply(e, base, 128, 96, 112)),
                inc="dv", tag=f"applyk{h}")

        AVC = [2048, 2304, 2560, 2816]

        def emit_attn(h):
            par = h % 2
            base = QK2 + par * 4096
            vxb = VX2 + par * 2112
            for p4 in range(8):
                qkoff = 0 if p4 < 4 else 128
                srcs = [bf(base + ((p4 % 4) * 4 + c) * 256 + qkoff,
                           base + ((p4 % 4) * 4 + c) * 256 + qkoff + 128)
                        for c in range(4)]
                dstb = (QT if p4 < 4 else KT) + (p4 % 4) * 512
                tpack(srcs, bf(dstb, dstb + 512),
                      [("dv", P.c(("applyq", h)) if p4 < 4 else P.c(("applyk", h)))],
                      f"qk{h}.{p4}")
            P.cp[("qktrl", h)] = P.c(("tp", gpk[0] - 1))
            cp_qktrev = P.c(("tpev", gpk[0] - 1))

            bidx = 0
            avb = {}

            def em_av(e, g, i):
                last = None
                c0 = (avb[(g, i)] % 3) * 512
                for j in range(max(i, 4 * g), 4 * g + 4):
                    cc = j - 4 * g
                    last = e.matmul(
                        pbig[:, AVC[cc]:AVC[cc] + 129],
                        bfar[:, ATG + c0 + cc * 128:ATG + c0 + (cc + 1) * 128],
                        bfar[:, vxb + i * 132:vxb + i * 132 + 129],
                        start=(i == 0 and cc % 2 == 0), stop=(i == j))
                return last

            def emit_av_for(g, i):
                bx = avb[(g, i)]
                w = [("dv", P.c(("sq2", h, bx)))]
                if i == 0:
                    pv = P.c(("aoevg", h, g - 1)) if g > 0 else P.c(("aoevg", h - 1, 3))
                    if pv:
                        w.append(("dv", pv))
                P.cp[("av", h, bx)] = P.step(
                    "te", (lambda e, g=g, i=i: em_av(e, g, i)), waits=w,
                    inc="pe", tag=f"av{h}.{g}.{i}")
                if i == 4 * g + 3:
                    def em_aoevg(e, g=g, vxb=vxb):
                        dn = _v(pbig[:, 2048:3072], [[256, 4], [1, 1]], doff=128)
                        rc = _v(stats[:, 240:244], [[1, 4], [1, 1]])
                        chain(e, "dv", e.reciprocal(rc, dn))
                        last = None
                        for cc in range(4):
                            j = 4 * g + cc
                            last = e.tensor_scalar(
                                bf(AOTM + j * 128, AOTM + (j + 1) * 128),
                                pbig[:, AVC[cc]:AVC[cc] + 128],
                                stats[:, 240 + cc:241 + cc], None, op0=AO.mult)
                        if g == 0:
                            chain(e, "dv", last)
                            last = e.tensor_copy(bfar[0:1, AOTM:AOTM + 128],
                                                 bfar[0:1, vxb:vxb + 128])
                        return last
                    P.cp[("aoevg", h, g)] = P.step(
                        "dv", em_aoevg, waits=[("pe", P.c(("av", h, bx)))],
                        inc="dv", tag=f"aoev{h}.{g}")

            for g in range(4):
                jmax = 4 * g + 3
                for i in range(jmax + 1):
                    w = []
                    if bidx == 0:
                        w.append(("dv", cp_qktrev))
                    if bidx < 2:
                        pv = P.c(("sq1", h - 1, 39))
                    else:
                        pv = P.c(("sq1", h, bidx - 2))
                    if pv:
                        w.append(("ac", pv))

                    def em_sc(e, g=g, i=i, bidx=bidx):
                        return e.matmul(
                            pbig[:, 1024 + (bidx % 2) * 512:1024 + (bidx % 2) * 512 + 512],
                            bfar[:, KT + i * 128:KT + (i + 1) * 128],
                            bfar[:, QT + g * 512:QT + (g + 1) * 512],
                            start=True, stop=True)
                    P.cp[("sc", h, bidx)] = P.step("te", em_sc, waits=w,
                                                   inc="pe", tag=f"sc{h}.{g}.{i}")
                    w = [("pe", P.c(("sc", h, bidx)))]
                    pv = P.c(("sq2", h, bidx - 3))
                    if pv:
                        w.append(("dv", pv))

                    def em_sq1(e, bidx=bidx):
                        return e.activation(
                            s2h[:, (bidx % 3) * 512:(bidx % 3) * 512 + 512],
                            pbig[:, 1024 + (bidx % 2) * 512:1024 + (bidx % 2) * 512 + 512],
                            AF.Square, scale=SQ_SCALE)
                    P.cp[("sq1", h, bidx)] = P.step("ac", em_sq1, waits=w,
                                                    inc="ac", tag=f"sq1.{h}.{bidx}")
                    diag = (4 * g <= i)

                    def em_sq2(e, bidx=bidx, diag=diag, cc=i - 4 * g):
                        sl = s2h[:, (bidx % 3) * 512:(bidx % 3) * 512 + 512]
                        if diag:
                            dsl = s2h[:, (bidx % 3) * 512 + cc * 128:
                                      (bidx % 3) * 512 + (cc + 1) * 128]
                            chain(e, "dv", e.tensor_tensor(dsl, dsl, tril16[:, :], op=AO.mult))
                        return e.tensor_tensor(
                            bfar[:, ATG + (bidx % 3) * 512:ATG + (bidx % 3 + 1) * 512],
                            sl, sl, op=AO.mult)
                    w = [("ac", P.c(("sq1", h, bidx)))]
                    pv = P.c(("av", h, bidx - 3))
                    if pv:
                        w.append(("pe", pv))
                    P.cp[("sq2", h, bidx)] = P.step("dv", em_sq2, waits=w,
                                                    inc="dv", tag=f"sq2.{h}.{bidx}")
                    avb[(g, i)] = bidx
                    if i >= 2:
                        emit_av_for(g, i - 2)
                    if i == jmax:
                        emit_av_for(g, i - 1)
                        emit_av_for(g, i)
                    bidx += 1
            P.cp[("scl", h)] = P.c(("sc", h, bidx - 1))
            P.cp[("avl", h)] = P.c(("av", h, bidx - 1))

            for p4 in range(4):
                srcs = [bf(AOTM + p4 * 512 + c * 128, AOTM + p4 * 512 + (c + 1) * 128)
                        for c in range(4)]
                tpack(srcs, aoT[:, h * 2048 + p4 * 512:h * 2048 + (p4 + 1) * 512],
                      [("dv", P.c(("aoevg", h, 3)))], f"ao{h}.{p4}")
            P.cp[("aotrl", h)] = P.c(("tp", gpk[0] - 1))
            P.cp[("aotrevl", h)] = P.c(("tpev", gpk[0] - 1))

        emit_proj(0)
        wq_dma_steps(1, [("pe", P.c(("projlast", 0)))])
        for h in range(H):
            emit_ew_start(h)
            hooks = {
                4: (lambda h=h: (emit_sq(h, "q"), emit_redq(h))),
                7: (lambda h=h: (emit_sq(h, "k"), emit_redk_stat(h))),
                11: (lambda h=h: (emit_statact(h), emit_apply(h))),
            }
            if h + 1 < H:
                emit_proj(h + 1, hooks)
            else:
                for fn in hooks.values():
                    fn()
            if h + 2 < H:
                wq_dma_steps(h + 2, [("pe", P.c(("projlast", h + 1)))])
            if h == H - 1:
                for c in range(NE):
                    P.step("gp", (lambda e, c=c: e.dma_start(
                        out=big[:, c * 2048:(c + 1) * 2048],
                        in_=woutd[c * 128:(c + 1) * 128, :])),
                        waits=([("pe", P.c(("projlast", H - 1)))] if c == 0 else ()),
                        inc="dwo", inc_amt=16, tag=f"wo{c}")
            emit_attn(h)

        P.step("gp", (lambda e: e.dma_start(
            out=bfar[:, UB:UB + 8192].rearrange("p (ec s) -> p ec s", ec=NE),
            in_=upermd.rearrange("(ec p) s -> p ec s", p=128))),
            waits=[("pe", P.c(("qktrl", H - 1)))],
            inc="duv", inc_amt=16, tag="udma")
        P.step("gp", (lambda e: e.dma_start(
            out=bfar[:, VB:VB + 8192].rearrange("p (sc e) -> p sc e", sc=4),
            in_=vflatd.rearrange("(sc p) e -> p sc e", p=128))),
            waits=[("pe", P.c(("avl", H - 1)))],
            inc="duv", inc_amt=16, tag="vdma")

        P.step("sy", (lambda e: e.dma_start(
            out=far[:, FB:FB + 2048], in_=xd[0:128, :])),
            waits=[("dv", P.c(("redk", H - 1)))], inc="dxr", inc_amt=16, tag="x0")
        P.step("sy", (lambda e: e.dma_start(
            out=far[:, FC:FC + 2048], in_=xd[128:256, :])),
            inc="dxr", inc_amt=16, tag="x1")

        def reg(t):
            b = FB if t % 2 == 0 else FC
            return far[:, b:b + 2048]

        h1bsc = bf(COSB, COSB + 2048)

        def emit_z(t):
            if t == 0:
                w = [("dwo", 256), ("dv", P.c(("aotrevl", H - 1)))]
            else:
                w = [("dv", P.c(("h1pre", t - 1)))]

            def em_z(e, t=t):
                last = None
                for ec in range(NE):
                    for nb in range(4):
                        last = e.matmul(
                            pbig[:, nb * 512:(nb + 1) * 512],
                            aoT[:, ec * 2048 + t * 128:ec * 2048 + t * 128 + 128],
                            big[:, ec * 2048 + nb * 512:ec * 2048 + (nb + 1) * 512],
                            start=(ec == 0), stop=(ec == NE - 1))
                return last
            P.cp[("z", t)] = P.step("te", em_z, waits=w, inc="pe", tag=f"z{t}")

        def emit_tile_chain(t):
            rg = reg(t)
            w = [("pe", P.c(("z", t))), ("dxr", 16 * (t + 1))]
            pv = P.c(("ln2sq", t - 2))
            if pv:
                w.append(("ac", pv))
            P.cp[("h1pre", t)] = P.step(
                "dv", (lambda e, rg=rg: e.tensor_tensor(rg, pbig[:, 0:2048], rg, op=AO.add)),
                waits=w, inc="dv", tag=f"h1pre{t}")
            P.cp[("l1red", t)] = P.step(
                "dv", (lambda e, rg=rg: e.tensor_reduce(gcol(G_L1S), rg, axis=AX.X, op=AO.add)),
                waits=[("dv", P.c(("h1pre", t)))], inc="dv", tag=f"l1red{t}")
            P.cp[("ln1sq", t)] = P.step(
                "ac", (lambda e, rg=rg: e.activation(h1bsc, rg, AF.Square,
                                                     accum_out=gcol(G_L1Q))),
                waits=[("dv", P.c(("h1pre", t)))], inc="ac", tag=f"ln1sq{t}")

            def em_ln1m(e):
                e.tensor_scalar(gcol(G_NMU1), gcol(G_L1S), -1.0 / E, None, op0=AO.mult)
                chain(e, "dv", e.tensor_scalar(gcol(G_TMP), gcol(G_L1Q), 1.0 / E, None, op0=AO.mult))
                chain(e, "dv", e.tensor_tensor(gcol(G_V1), gcol(G_NMU1), gcol(G_NMU1), op=AO.mult))
                return e.tensor_tensor(gcol(G_V1), gcol(G_TMP), gcol(G_V1), op=AO.subtract)
            P.cp[("ln1m", t)] = P.step("dv", em_ln1m,
                                       waits=[("ac", P.c(("ln1sq", t))),
                                              ("dv", P.c(("l1red", t)))],
                                       inc="dv", tag=f"ln1m{t}")

            def em_ln1a(e):
                chain(e, "ac", e.activation(gcol(G_TMP), gcol(G_V1), AF.Ln, bias=epsc[:, 0:1]))
                return e.activation(gcol(G_RS1), gcol(G_TMP), AF.Exp, scale=-0.5)
            P.cp[("ln1a", t)] = P.step("ac", em_ln1a,
                                       waits=[("dv", P.c(("ln1m", t)))],
                                       inc="ac", tag=f"ln1a{t}")

            def em_h1ap(e, rg=rg):
                e.tensor_scalar(bf(H1B, H1B + 2048), rg, gcol(G_NMU1), gcol(G_RS1),
                                op0=AO.add, op1=AO.mult)
                return e.tensor_scalar(rg, rg, gcol(G_NMU1), gcol(G_RS1),
                                       op0=AO.add, op1=AO.mult)
            P.cp[("h1", t)] = P.step("dv", em_h1ap,
                                     waits=[("ac", P.c(("ln1a", t)))],
                                     inc="dv", tag=f"h1ap{t}")

        def emit_h1tp(t):
            for p4 in range(4):
                srcs = [bf(H1B + p4 * 512 + c * 128, H1B + p4 * 512 + (c + 1) * 128)
                        for c in range(4)]
                tpack(srcs, bf(H1T + p4 * 512, H1T + (p4 + 1) * 512),
                      [("dv", P.c(("h1", t)))], f"h1{t}.{p4}")
            P.cp[("h1trev", t)] = P.c(("tpev", gpk[0] - 1))

        def emit_schu(t):
            w = [("dv", P.c(("h1trev", t)))]
            if t == 0:
                w.append(("duv", 16))
            else:
                w += [("dv", P.c(("ffnev", t - 1, 3))), ("ac", P.c(("e1", t - 1)))]

            def em_schu(e):
                last = None
                for ec in range(NE):
                    e.matmul(pbig[:, 2048:2048 + S],
                             bf(H1T + ec * 128, H1T + (ec + 1) * 128),
                             wrtb[:, ec * S:(ec + 1) * S],
                             start=(ec == 0), stop=(ec == NE - 1))
                    last = e.matmul(pbig[:, 2560:3072],
                                    bf(H1T + ec * 128, H1T + (ec + 1) * 128),
                                    bfar[:, UB + ec * 512:UB + (ec + 1) * 512],
                                    start=(ec == 0), stop=(ec == NE - 1))
                return last
            P.cp[("schu", t)] = P.step("te", em_schu, waits=w, inc="pe",
                                       tag=f"schu{t}")

        def emit_gate(t):
            def em_gatea(e):
                psc = pbig[:, 2048:2048 + S]
                chain(e, "dv", e.reduce_max(gcol(G_M1), psc, axis=AX.X))
                e.tensor_scalar(gcol(G_NM1), gcol(G_M1), -1.0, None, op0=AO.mult)
                chain(e, "dv", e.tensor_copy(gat[:, G_SCF:G_SCF + S], psc))
                chain(e, "dv", e.tensor_scalar(gat[:, G_MK1:G_MK1 + S], gat[:, G_SCF:G_SCF + S],
                                gcol(G_M1), None, op0=AO.is_ge))
                chain(e, "dv", e.scalar_tensor_tensor(gat[:, G_SC2:G_SC2 + S],
                                       gat[:, G_MK1:G_MK1 + S], -1e30,
                                       gat[:, G_SCF:G_SCF + S],
                                       op0=AO.mult, op1=AO.add))
                chain(e, "dv", e.reduce_max(gcol(G_M2), gat[:, G_SC2:G_SC2 + S], axis=AX.X))
                return e.tensor_scalar(gat[:, G_MK:G_MK + S], gat[:, G_SCF:G_SCF + S],
                                       gcol(G_M2), None, op0=AO.is_ge)
            P.cp[("gatea", t)] = P.step("dv", em_gatea,
                                        waits=[("pe", P.c(("schu", t)))],
                                        inc="dv", tag=f"gatea{t}")
            P.cp[("e1", t)] = P.step(
                "ac", (lambda e: e.activation(gat[:, G_E1:G_E1 + S],
                                              pbig[:, 2048:2048 + S], AF.Exp,
                                              bias=gcol(G_NM1))),
                waits=[("dv", P.c(("gatea", t)))], inc="ac", tag=f"e1{t}")

            def em_gateb(e):
                e1v = gat[:, G_E1:G_E1 + S]
                e.tensor_reduce(gcol(G_Z), e1v, axis=AX.X, op=AO.add)
                chain(e, "dv", e.tensor_tensor(gat[:, G_G0:G_G0 + S], e1v, gat[:, G_MK:G_MK + S], op=AO.mult))
                chain(e, "dv", e.tensor_reduce(gcol(G_S0), gat[:, G_G0:G_G0 + S], axis=AX.X, op=AO.add))
                chain(e, "dv", e.scalar_tensor_tensor(gcol(G_DG), gcol(G_Z), 1e-8, gcol(G_S0),
                                       op0=AO.mult, op1=AO.add))
                chain(e, "dv", e.reciprocal(gcol(G_RG), gcol(G_DG)))
                chain(e, "dv", e.tensor_scalar(gat[:, G_GN:G_GN + S], gat[:, G_G0:G_G0 + S],
                                gcol(G_RG), None, op0=AO.mult))
                chain(e, "dv", e.tensor_tensor(gat[:, G_SCF:G_SCF + S], gat[:, G_GN:G_GN + S],
                                awb[:, :], op=AO.mult))
                chain(e, "dv", e.tensor_reduce(gcol(G_GAW), gat[:, G_SCF:G_SCF + S], axis=AX.X, op=AO.add))
                return e.tensor_scalar(gcol(G_GF), gcol(G_GAW), 0.2, 0.9,
                                       op0=AO.mult, op1=AO.add)
            P.cp[("gateb", t)] = P.step("dv", em_gateb,
                                        waits=[("ac", P.c(("e1", t)))],
                                        inc="dv", tag=f"gateb{t}")

            def em_wg(e):
                hu = _v(pbig[:, 2560:3072], [[16, S], [1, 16]])
                gn = _v(gat[:, G_GN:G_GN + S], [[1, S], [0, 16]])
                wo = _v(bf(WGB, WGB + 512), [[16, S], [1, 16]])
                return e.tensor_tensor(wo, hu, gn, op=AO.mult)
            P.cp[("wg", t)] = P.step("dv", em_wg, inc="dv", tag=f"wg{t}")

        def emit_wgtp(t):
            srcs = [bf(WGB + c * 128, WGB + (c + 1) * 128) for c in range(4)]
            tpack(srcs, bf(WGT, WGT + 512), [("dv", P.c(("wg", t)))], f"wg{t}")
            P.cp[("wgtrev", t)] = P.c(("tpev", gpk[0] - 1))

        def emit_ffn(t):
            rg = reg(t)
            for q in range(4):
                w = [("dv", P.c(("wgtrev", t)))]
                if q == 0:
                    w.append(("ac", P.c(("e1", t))))
                    if t == 0:
                        w.append(("duv", 32))
                elif q >= 2:
                    w.append(("dv", P.c(("ffnev", t, q - 2))))

                def em_q(e, q=q):
                    last = None
                    pcol = 2048 + (q % 2) * 512
                    for src in range(4):
                        last = e.matmul(
                            pbig[:, pcol:pcol + 512],
                            bf(WGT + src * 128, WGT + (src + 1) * 128),
                            bfar[:, VB + src * 2048 + q * 512:VB + src * 2048 + (q + 1) * 512],
                            start=(src == 0), stop=(src == 3))
                    return last
                P.cp[("ffnq", t, q)] = P.step("te", em_q, waits=w, inc="pe",
                                              tag=f"ffnq{t}.{q}")
                P.cp[("ffnev", t, q)] = P.step(
                    "dv", (lambda e, q=q, rg=rg: e.scalar_tensor_tensor(
                        rg[:, q * 512:(q + 1) * 512],
                        pbig[:, 2048 + (q % 2) * 512:2048 + (q % 2) * 512 + 512],
                        gcol(G_GF), rg[:, q * 512:(q + 1) * 512],
                        op0=AO.mult, op1=AO.add)),
                    waits=[("pe", P.c(("ffnq", t, q)))], inc="dv",
                    tag=f"ffnev{t}.{q}")

        def emit_ln2(t):
            rg = reg(t)
            P.cp[("l2red", t)] = P.step(
                "dv", (lambda e, rg=rg: e.tensor_reduce(gcol(G_L2S), rg, axis=AX.X, op=AO.add)),
                inc="dv", tag=f"l2red{t}")
            P.cp[("ln2sq", t)] = P.step(
                "ac", (lambda e, rg=rg: e.activation(h1bsc, rg, AF.Square,
                                                     accum_out=gcol(G_L2Q))),
                waits=[("dv", P.c(("ffnev", t, 3)))], inc="ac", tag=f"ln2sq{t}")

            def em_ln2m(e):
                e.tensor_scalar(gcol(G_NMU2), gcol(G_L2S), -1.0 / E, None, op0=AO.mult)
                chain(e, "pl", e.tensor_scalar(gcol(G_TMP2), gcol(G_L2Q), 1.0 / E, None, op0=AO.mult))
                chain(e, "pl", e.tensor_tensor(gcol(G_V2), gcol(G_NMU2), gcol(G_NMU2), op=AO.mult))
                return e.tensor_tensor(gcol(G_V2), gcol(G_TMP2), gcol(G_V2), op=AO.subtract)
            P.cp[("ln2m", t)] = P.step("gp", em_ln2m,
                                       waits=[("ac", P.c(("ln2sq", t))),
                                              ("dv", P.c(("l2red", t)))],
                                       inc="pl", tag=f"ln2m{t}")

            def em_ln2a(e):
                chain(e, "ac", e.activation(gcol(G_TMP2), gcol(G_V2), AF.Ln, bias=epsc[:, 0:1]))
                return e.activation(gcol(G_RS2), gcol(G_TMP2), AF.Exp, scale=-0.5)
            P.cp[("ln2a", t)] = P.step("ac", em_ln2a,
                                       waits=[("pl", P.c(("ln2m", t)))],
                                       inc="ac", tag=f"ln2a{t}")
            P.cp[("y", t)] = P.step(
                "gp", (lambda e, rg=rg: e.tensor_scalar(
                    rg, rg, gcol(G_NMU2), gcol(G_RS2), op0=AO.add, op1=AO.mult)),
                waits=[("ac", P.c(("ln2a", t)))], inc="pl", tag=f"yap{t}")
            P.step("sy", (lambda e, t=t, rg=rg: e.dma_start(
                out=yd[t * 128:(t + 1) * 128, :], in_=rg)),
                waits=[("pl", P.c(("y", t)))], inc="dy", inc_amt=16, tag=f"ydma{t}")
            if t + 2 < NT:
                P.step("sy", (lambda e, t=t: e.dma_start(
                    out=reg(t), in_=xd[(t + 2) * 128:(t + 3) * 128, :])),
                    waits=[("dy", 16 * (t + 1))], inc="dxr", inc_amt=16,
                    tag=f"x{t + 2}")

        emit_z(0)
        emit_tile_chain(0)
        for t in range(NT):
            emit_h1tp(t)
            emit_schu(t)
            if t + 1 < NT:
                emit_z(t + 1)
            emit_gate(t)
            emit_wgtp(t)
            if t + 1 < NT:
                emit_tile_chain(t + 1)
            emit_ffn(t)
            emit_ln2(t)

        P.step("sy", (lambda e: e.nop()), waits=[("dy", 16 * NT)], tag="fin")

        steps = P.steps
        if cut is not None:
            idx = next(i for i, st in enumerate(steps) if st["tag"] == cut)
            steps = steps[:idx + 1]
            cnts = {}
            for st in steps:
                if st["inc"]:
                    cnts[st["inc"]] = cnts.get(st["inc"], 0) + st["inc_amt"]
            barrier = [(n, v) for n, v in cnts.items()
                       if n in ("pe", "dv", "ac", "pl")]
            ndma = 0
            env = dict(bfar=bfar, far=far, big=big, aoT=aoT, s2h=s2h, pbig=pbig,
                       stats=stats, gat=gat, awb=awb)
            for name, apfn, shape, dt in taps:
                ap = apfn(env)
                steps.append(dict(engine="sy",
                                  emit=(lambda e, o=tap_outs[name], i=ap:
                                        e.dma_start(out=o, in_=i)),
                                  waits=barrier, inc="dtap", inc_amt=16,
                                  tag="tap" + name))
                ndma += 1
            steps.append(dict(engine="sy", emit=(lambda e: e.nop()),
                              waits=[("dtap", 16 * ndma)], inc=None, inc_amt=1,
                              tag="tapfin"))

        def walker(key):
            def run(eng):
                for st in steps:
                    if st["engine"] != key:
                        continue
                    for s, v in st["waits"]:
                        eng.wait_ge(sems[s], v)
                    last = st["emit"](eng)
                    if st["inc"] is not None:
                        assert last is not None, st["tag"]
                        last.then_inc(sems[st["inc"]], st["inc_amt"])
            return run

        block.tensor(walker("te"))
        block.vector(walker("dv"))
        block.scalar(walker("ac"))
        block.gpsimd(walker("gp"))
        block.sync(walker("sy"))
    return nc


def _host_inputs(L0_attn_x, L0_attn_w_qkv, L0_attn_w_out, L0_attn_ln_scale,
                 L0_attn_ln_bias, L0_ffn_Wr, L0_ffn_U, L0_ffn_V, L0_ffn_attr,
                 L0_ffn_Wa):
    x = np.asarray(L0_attn_x, dtype=np.float32)
    wqkv = np.ascontiguousarray(np.asarray(L0_attn_w_qkv, np.float32))
    wout = np.ascontiguousarray(np.asarray(L0_attn_w_out, np.float32))
    uperm = np.ascontiguousarray(
        np.asarray(L0_ffn_U, np.float32).transpose(1, 0, 2).reshape(E, S * R))
    vflat = np.ascontiguousarray(np.asarray(L0_ffn_V, np.float32).reshape(S * R, E))
    wrt = np.ascontiguousarray(np.asarray(L0_ffn_Wr, np.float32).T)
    attrT = np.ascontiguousarray(np.asarray(L0_ffn_attr, np.float32).T)
    waT = np.ascontiguousarray(np.asarray(L0_ffn_Wa, np.float32).reshape(S, 1))
    inv = (1.0 / (10000.0 ** (np.arange(0, D, 2, dtype=np.float32) /
                              np.float32(D)))).astype(np.float32)
    tt = np.arange(T, dtype=np.float32)
    ang = tt[:, None] * inv[None, :]
    cos = np.cos(ang).astype(np.float32)
    sin = np.sin(ang).astype(np.float32)
    cosr = np.ascontiguousarray(cos.reshape(NT, 128, 64).transpose(1, 0, 2)
                                .reshape(128, NT * 64))
    sinr = np.ascontiguousarray(sin.reshape(NT, 128, 64).transpose(1, 0, 2)
                                .reshape(128, NT * 64))
    scb = np.ascontiguousarray(np.broadcast_to(
        np.asarray(L0_attn_ln_scale, np.float32)[None, :], (128, D)))
    bib = np.ascontiguousarray(np.broadcast_to(
        np.asarray(L0_attn_ln_bias, np.float32)[None, :], (128, D)))
    trilm = np.ascontiguousarray(np.triu(np.ones((128, 128), np.float32)))
    ident = np.eye(128, dtype=ml_dtypes.bfloat16)
    shared = dict(wqkv=wqkv, wout=wout, uperm=uperm, vflat=vflat, wrt=wrt,
                  attrT=attrT, waT=waT, cosr=cosr, sinr=sinr, scb=scb, bib=bib,
                  trilm=trilm, ident=ident)
    in_maps = []
    for b in range(8):
        m = dict(shared)
        m["x"] = np.ascontiguousarray(x[b])
        m["xT"] = np.ascontiguousarray(x[b].T)
        in_maps.append(m)
    return in_maps


def kernel(**inputs):
    in_maps = _host_inputs(**inputs)
    nc = build()
    res = run_bass_kernel_spmd(nc, in_maps, core_ids=list(range(8)))
    return np.stack([np.asarray(r["y"], dtype=np.float32) for r in res.results],
                    axis=0)


# version 05
# speedup vs baseline: 1.2119x; 1.2119x over previous
import dataclasses
from contextlib import ExitStack

import numpy as np
import ml_dtypes
import concourse.bass as bass
import concourse.mybir as mybir
from concourse.bass_utils import run_bass_kernel_spmd
from concourse.alu_op_type import AluOpType as AO

F32 = mybir.dt.float32
BF16 = mybir.dt.bfloat16
FP16 = mybir.dt.float16
AF = mybir.ActivationFunctionType
AX = mybir.AxisListType

T = 2048
E = 2048
NT = 16
NE = 16
H = 16
D = 128
S = 32
R = 16
LN_EPS = 1e-5
SQ_SCALE = 1.0 / 16.0


def _v(ap2, dims, doff=0):
    return dataclasses.replace(
        ap2, ap=[list(ap2.ap[0])] + [list(d) for d in dims],
        offset=ap2.offset + doff)


class Prog:
    def __init__(self):
        self.steps = []
        self.cnt = {}
        self.cp = {}

    def step(self, engine, emit, waits=(), inc=None, inc_amt=1, tag=""):
        ws = [(s, v) for (s, v) in waits if v and v > 0]
        self.steps.append(dict(engine=engine, emit=emit, waits=ws, inc=inc,
                               inc_amt=inc_amt, tag=tag))
        if inc is not None:
            self.cnt[inc] = self.cnt.get(inc, 0) + inc_amt
            return self.cnt[inc]
        return None

    def c(self, key):
        return self.cp.get(key, 0)


def build(cut=None, taps=()):
    nc = bass.Bass("TRN2", target_bir_lowering=False, debug=False)

    xd = nc.dram_tensor("x", [T, E], F32, kind="ExternalInput").ap()
    xTd = nc.dram_tensor("xT", [E, T], F32, kind="ExternalInput").ap()
    wqkvd = nc.dram_tensor("wqkv", [E, 3 * E], F32, kind="ExternalInput").ap()
    woutd = nc.dram_tensor("wout", [E, E], F32, kind="ExternalInput").ap()
    upermd = nc.dram_tensor("uperm", [E, S * R], F32, kind="ExternalInput").ap()
    vflatd = nc.dram_tensor("vflat", [S * R, E], F32, kind="ExternalInput").ap()
    wrtd = nc.dram_tensor("wrt", [E, S], F32, kind="ExternalInput").ap()
    attrTd = nc.dram_tensor("attrT", [S, S], F32, kind="ExternalInput").ap()
    waTd = nc.dram_tensor("waT", [S, 1], F32, kind="ExternalInput").ap()
    cosd = nc.dram_tensor("cosr", [128, NT * 64], F32, kind="ExternalInput").ap()
    sind = nc.dram_tensor("sinr", [128, NT * 64], F32, kind="ExternalInput").ap()
    scbd = nc.dram_tensor("scb", [128, D], F32, kind="ExternalInput").ap()
    bibd = nc.dram_tensor("bib", [128, D], F32, kind="ExternalInput").ap()
    trild = nc.dram_tensor("trilm", [128, 128], F32, kind="ExternalInput").ap()
    identd = nc.dram_tensor("ident", [128, 128], BF16, kind="ExternalInput").ap()
    yd = nc.dram_tensor("y", [T, E], F32, kind="ExternalOutput").ap()
    tap_outs = {}
    for name, _ap, shape, dt in taps:
        tap_outs[name] = nc.dram_tensor("tap_" + name, shape, dt,
                                        kind="ExternalOutput").ap()

    with ExitStack() as es:
        def sbt(name, shape, dt):
            return es.enter_context(nc.sbuf_tensor(name, shape, dt))

        big = sbt("big", [128, 32768], BF16)
        aoT = sbt("aoT", [128, 32768], BF16)
        bfar = sbt("bfar", [128, 28544], BF16)
        far = sbt("far", [128, 4352], F32)
        s2h = sbt("s2h", [128, 1536], FP16)
        tril16 = sbt("tril16", [128, 128], FP16)
        idb = sbt("idb", [128, 128], BF16)
        wrtb = sbt("wrtb", [128, NE * S], BF16)
        awb = sbt("awb", [128, S], F32)
        attrb = sbt("attrb", [S, S], BF16)
        watb = sbt("watb", [S, 1], BF16)
        awf = sbt("awf", [S, 2], F32)
        awbf = sbt("awbf", [S, 1], BF16)
        awrow = sbt("awrow", [1, S], BF16)
        ones1 = sbt("ones1", [1, 128], BF16)
        epsc = sbt("epsc", [128, 1], F32)

        QK2 = 0
        QT, KT = 8192, 10240
        VX2 = 12288
        ATG, AOTM, WQ = 16512, 18048, 20096
        COSB, SINB, SCB, BIB = 26240, 27264, 28288, 28416
        UB, VB, H1B, H1T, WGB, WGT = 0, 8192, 16512, 19072, 21120, 21632
        FB, FC, GAT = 0, 2048, 4096

        pbig = es.enter_context(nc.psum_tensor("pbig", [128, 3072], F32))
        pT = es.enter_context(nc.psum_tensor("pT", [128, 2048], BF16))

        semnames = ["pe", "dv", "ac", "pl", "dxt", "dwq", "dcn", "dwo", "duv",
                    "dxr", "dy", "dtap", "cdv", "cac", "cpl"]
        sems = {n: es.enter_context(nc.semaphore(n)) for n in semnames}
        block = es.enter_context(nc.Block())

        P = Prog()
        ccnt = {"dv": 0, "ac": 0, "pl": 0}

        def chain(e, key, ins):
            ccnt[key] += 1
            ins.then_inc(sems["c" + key], 1)
            e.wait_ge(sems["c" + key], ccnt[key])
            return ins

        def bf(c0, c1):
            return bfar[:, c0:c1]

        stats = far[:, GAT:GAT + 256]
        gat = far[:, GAT:GAT + 256]
        (G_M1, G_NM1, G_M2, G_Z, G_S0, G_DG, G_RG, G_GAW, G_GF,
         G_L1S, G_L1Q, G_NMU1, G_RS1, G_L2S, G_L2Q, G_NMU2, G_RS2,
         G_V1, G_V2, G_TMP) = range(20)
        G_TMP2 = 20
        G_E1, G_MK1, G_SCF, G_SC2, G_MK, G_G0, G_GN = 32, 64, 96, 128, 160, 192, 224

        def gcol(c):
            return gat[:, c:c + 1]

        P.step("sy", lambda e: e.dma_start(out=idb[:, :], in_=identd),
               inc="dcn", inc_amt=16, tag="idb")
        gp_consts = [
            (bf(COSB, COSB + 1024), cosd),
            (bf(SINB, SINB + 1024), sind),
            (bf(SCB, SCB + 128), scbd),
            (bf(BIB, BIB + 128), bibd),
            (tril16[:, :], trild),
            (attrb[:, :], attrTd),
            (watb[:, :], waTd),
            (wrtb[:, :].rearrange("p (ec s) -> p ec s", ec=NE),
             wrtd.rearrange("(ec p) s -> p ec s", p=128)),
        ]
        for (o, i) in gp_consts:
            P.step("gp", (lambda e, o=o, i=i: e.dma_start(out=o, in_=i)),
                   inc="dcn", inc_amt=16, tag="const")
        CN_ALL = P.cnt["dcn"]
        for c in range(NE):
            P.step("gp", (lambda e, c=c: e.dma_start(
                out=big[:, c * 2048:(c + 1) * 2048],
                in_=xTd[c * 128:(c + 1) * 128, :])),
                inc="dxt", inc_amt=16, tag=f"xt{c}")

        def wq_dma_steps(h, waits):
            wqv = bfar[:, WQ:WQ + 6144].rearrange("p (ec c) -> p ec c", c=384)
            for qkv in range(3):
                off = qkv * E + h * 128
                P.step("gp", (lambda e, qkv=qkv, off=off, wqv=wqv: e.dma_start(
                    out=wqv[:, :, qkv * 128:(qkv + 1) * 128],
                    in_=wqkvd[:, off:off + 128].rearrange("(ec p) d -> p ec d", p=128))),
                    waits=waits, inc="dwq", inc_amt=16, tag=f"wq{h}.{qkv}")
        wq_dma_steps(0, ())

        def em_setup(e):
            for par in range(2):
                vxo = _v(bf(VX2 + par * 2112, VX2 + par * 2112 + 2112),
                         [[132, NT], [1, 1]], doff=128)
                e.memset(vxo, 1.0)
            e.memset(epsc[:, :], LN_EPS)
            return e.memset(ones1[:, :], 1.0)
        P.step("dv", em_setup, inc="dv", tag="ones")

        cp_awmm = P.step("te", lambda e: e.matmul(
            pbig[0:S, 0:1], attrb[:, :], watb[:, :], start=True, stop=True),
            waits=[("dcn", CN_ALL)], inc="pe", tag="awmm")
        cp_awexp = P.step("ac", lambda e: e.activation(
            awf[:, 0:1], pbig[0:S, 0:1], AF.Exp, scale=-1.0),
            waits=[("pe", cp_awmm)], inc="ac", tag="awexp")

        def em_awsig(e):
            chain(e, "dv", e.tensor_scalar(awf[:, 1:2], awf[:, 0:1], 1.0, None, op0=AO.add))
            chain(e, "dv", e.reciprocal(awf[:, 0:1], awf[:, 1:2]))
            return e.tensor_copy(awbf[:, :], awf[:, 0:1])
        cp_awsig = P.step("dv", em_awsig, waits=[("ac", cp_awexp)], inc="dv",
                          tag="awsig")
        cp_awtr = P.step("te", lambda e: e.transpose(
            pT[0:1, 0:S], awbf[:, :], idb[0:S, 0:S]),
            waits=[("dv", cp_awsig)], inc="pe", tag="awtr")
        cp_awrow = P.step("dv", lambda e: e.tensor_copy(awrow[:, :], pT[0:1, 0:S]),
                          waits=[("pe", cp_awtr)], inc="dv", tag="awrowcp")
        cp_awbc = P.step("te", lambda e: e.matmul(
            pbig[:, 64:64 + S], ones1[:, :], awrow[:, :], start=True, stop=True),
            waits=[("dv", cp_awrow)], inc="pe", tag="awbc")
        cp_awbcp = P.step("dv", lambda e: e.tensor_copy(awb[:, :], pbig[:, 64:64 + S]),
                          waits=[("pe", cp_awbc)], inc="dv", tag="awbcp")

        gpk = [0]
        g16 = [0]

        def tpack(srcs, dst, waits, tag):
            g = gpk[0]
            slot = pT[:, (g % 2) * 1024:(g % 2) * 1024 + 512]
            w = list(waits)
            pv = P.c(("tpev", g - 2))
            if pv:
                w.append(("dv", pv))

            def em(e, srcs=srcs, slot=slot):
                last = None
                for c, s in enumerate(srcs):
                    last = e.transpose(slot[:, c * 128:(c + 1) * 128], s, idb[:, :])
                return last
            P.cp[("tp", g)] = P.step("te", em, waits=w, inc="pe", tag="tp" + tag)
            P.cp[("tpev", g)] = P.step(
                "dv", (lambda e, slot=slot, dst=dst: e.tensor_copy(dst, slot)),
                waits=[("pe", P.c(("tp", g)))], inc="dv", tag="tpev" + tag)
            gpk[0] = g + 1

        def emit_proj(hh, hooks=()):
            hooks = dict(hooks)
            par = hh % 2
            qkb = QK2 + par * 4096
            vxb = VX2 + par * 2112
            for k in range(NT):
                g = g16[0]
                slot = g % 2
                psv = pbig[:, slot * 512:(slot + 1) * 512]
                w = []
                if k == 0:
                    w.append(("dwq", 48 * (hh + 1)))
                    if hh == 0:
                        w += [("dxt", 256), ("dv", cp_awbcp)]
                ev = P.c(("pevac", g - 2))
                if ev:
                    w.append(("ac", ev))

                def em_p(e, k=k, psv=psv):
                    last = None
                    for ec in range(NE):
                        last = e.matmul(
                            psv[:, 0:384],
                            big[:, ec * 2048 + k * 128:ec * 2048 + k * 128 + 128],
                            bfar[:, WQ + ec * 384:WQ + (ec + 1) * 384],
                            start=(ec == 0), stop=(ec == NE - 1))
                    return last
                P.cp[("p16", hh, k)] = P.step("te", em_p, waits=w, inc="pe",
                                              tag=f"p16.{hh}.{k}")

                def em_ev(e, k=k, psv=psv, qkb=qkb, vxb=vxb):
                    e.copy(bf(qkb + k * 256, qkb + (k + 1) * 256), psv[:, 0:256])
                    return e.copy(bf(vxb + k * 132, vxb + k * 132 + 128),
                                  psv[:, 256:384])
                P.cp[("pevac", g)] = P.step(
                    "ac", em_ev, waits=[("pe", P.c(("p16", hh, k)))],
                    inc="ac", tag=f"pev{hh}.{k}")
                g16[0] = g + 1
                if k in hooks:
                    hooks[k]()
            P.cp[("projlast", hh)] = P.c(("p16", hh, NT - 1))
            P.cp[("pevlast", hh)] = P.c(("pevac", g16[0] - 1))

        def rope_emit(e, base, qkoff, s0, s1, s2c, ck):
            q1 = _v(bf(base, base + 4096), [[256, NT], [1, 64]], doff=qkoff)
            q2 = _v(bf(base, base + 4096), [[256, NT], [1, 64]], doff=qkoff + 64)
            cosv = _v(bf(COSB, COSB + 1024), [[64, NT], [1, 64]])
            sinv = _v(bf(SINB, SINB + 1024), [[64, NT], [1, 64]])
            ua = _v(bf(s0, s0 + 1024), [[64, NT], [1, 64]])
            ua2 = _v(bf(s1, s1 + 1024), [[64, NT], [1, 64]])
            ub = _v(bf(s2c, s2c + 1024), [[64, NT], [1, 64]])
            e.tensor_tensor(ua, q1, cosv, op=AO.mult)
            e.tensor_tensor(ua2, q1, sinv, op=AO.mult)
            chain(e, ck, e.tensor_tensor(ub, q2, sinv, op=AO.mult))
            chain(e, ck, e.tensor_tensor(q1, ua, ub, op=AO.subtract))
            chain(e, ck, e.tensor_tensor(ub, q2, cosv, op=AO.mult))
            return e.tensor_tensor(q2, ua2, ub, op=AO.add)

        scbv = _v(bf(SCB, SCB + 128), [[0, NT], [1, 128]])
        bibv = _v(bf(BIB, BIB + 128), [[0, NT], [1, 128]])
        sqsv = _v(far[:, FB:FB + 2048], [[128, NT], [1, 128]])

        def em_apply(e, base, qkoff, muc, rsc):
            ins = None
            for tt in range(NT):
                sl = bf(base + tt * 256 + qkoff, base + tt * 256 + qkoff + 128)
                ins = e.tensor_scalar(sl, sl,
                                      stats[:, muc + tt:muc + tt + 1],
                                      stats[:, rsc + tt:rsc + tt + 1],
                                      op0=AO.add, op1=AO.mult)
            chain(e, "dv", ins)
            hv = _v(bf(base, base + 4096), [[256, NT], [1, 128]], doff=qkoff)
            chain(e, "dv", e.tensor_tensor(hv, hv, scbv, op=AO.mult))
            return e.tensor_tensor(hv, hv, bibv, op=AO.add)

        def emit_ew_start(h):
            par = h % 2
            base = QK2 + par * 4096
            if h == H - 1:
                sq_ = (WQ, WQ + 1024, WQ + 2048)
                sk_ = (WQ + 3072, WQ + 4096, WQ + 5120)
                wq_ = [("ac", P.c(("pevlast", h))), ("pe", P.c(("projlast", h)))]
                wk_ = list(wq_)
            else:
                sq_ = (QT, QT + 1024, KT)
                sk_ = (AOTM, AOTM + 1024, ATG)
                wq_ = [("ac", P.c(("pevlast", h)))]
                scl = P.c(("scl", h - 1))
                if scl:
                    wq_.append(("pe", scl))
                wk_ = [("ac", P.c(("pevlast", h)))]
                atr = P.c(("aotrl", h - 1))
                if atr:
                    wk_.append(("pe", atr))
            if h == 0:
                wq_.append(("dcn", CN_ALL))
                wk_.append(("dcn", CN_ALL))
            P.cp[("ropeq", h)] = P.step(
                "dv", (lambda e, base=base, s=sq_: rope_emit(e, base, 0, s[0], s[1], s[2], "dv")),
                waits=wq_, inc="dv", tag=f"ropeq{h}")
            P.cp[("ropek", h)] = P.step(
                "gp", (lambda e, base=base, s=sk_: rope_emit(e, base, 128, s[0], s[1], s[2], "pl")),
                waits=wk_, inc="pl", tag=f"ropek{h}")

        def em_red2(e, base, qkoff, c0, c1):
            src = _v(bf(base, base + 4096), [[256, NT], [1, 128]], doff=qkoff)
            e.tensor_reduce(_v(stats[:, c0:c0 + 16], [[1, 16], [1, 1]]),
                            src, axis=AX.X, op=AO.add)
            return e.tensor_reduce(_v(stats[:, c1:c1 + 16], [[1, 16], [1, 1]]),
                                   sqsv, axis=AX.X, op=AO.add)

        def emit_sq(h, which):
            par = h % 2
            base = QK2 + par * 4096
            qkoff = 0 if which == "q" else 128
            src = _v(bf(base, base + 4096), [[256, NT], [1, 128]], doff=qkoff)
            if which == "q":
                w = [("dv", P.c(("ropeq", h)))]
            else:
                w = [("pl", P.c(("ropek", h))), ("dv", P.c(("redq", h)))]
            P.cp[("sq" + which, h)] = P.step(
                "ac", (lambda e, src=src: e.activation(sqsv, src, AF.Square)),
                waits=w, inc="ac", tag=f"sq{which}{h}")

        def emit_redq(h):
            par = h % 2
            base = QK2 + par * 4096
            P.cp[("redq", h)] = P.step(
                "dv", (lambda e, base=base: em_red2(e, base, 0, 0, 16)),
                waits=[("ac", P.c(("sqq", h)))], inc="dv", tag=f"redq{h}")

        def emit_redk_stat(h):
            par = h % 2
            base = QK2 + par * 4096
            P.cp[("redk", h)] = P.step(
                "dv", (lambda e, base=base: em_red2(e, base, 128, 32, 48)),
                waits=[("ac", P.c(("sqk", h)))], inc="dv", tag=f"redk{h}")

            def em_statm(e):
                e.tensor_scalar(stats[:, 64:80], stats[:, 0:16], -1.0 / D, None, op0=AO.mult)
                e.tensor_scalar(stats[:, 96:112], stats[:, 32:48], -1.0 / D, None, op0=AO.mult)
                e.tensor_scalar(stats[:, 128:144], stats[:, 16:32], 1.0 / D, None, op0=AO.mult)
                chain(e, "dv", e.tensor_scalar(stats[:, 144:160], stats[:, 48:64], 1.0 / D, None, op0=AO.mult))
                chain(e, "dv", e.tensor_tensor(stats[:, 160:176], stats[:, 64:80], stats[:, 64:80], op=AO.mult))
                chain(e, "dv", e.tensor_tensor(stats[:, 128:144], stats[:, 128:144], stats[:, 160:176], op=AO.subtract))
                chain(e, "dv", e.tensor_tensor(stats[:, 160:176], stats[:, 96:112], stats[:, 96:112], op=AO.mult))
                return e.tensor_tensor(stats[:, 144:160], stats[:, 144:160], stats[:, 160:176], op=AO.subtract)
            P.cp[("statm", h)] = P.step("dv", em_statm,
                                        waits=[("dv", P.c(("redk", h)))],
                                        inc="dv", tag=f"statm{h}")

        def emit_statact(h):
            def em(e):
                chain(e, "ac", e.activation(stats[:, 160:176], stats[:, 128:144], AF.Ln, bias=epsc[:, 0:1]))
                chain(e, "ac", e.activation(stats[:, 80:96], stats[:, 160:176], AF.Exp, scale=-0.5))
                chain(e, "ac", e.activation(stats[:, 160:176], stats[:, 144:160], AF.Ln, bias=epsc[:, 0:1]))
                return e.activation(stats[:, 112:128], stats[:, 160:176], AF.Exp, scale=-0.5)
            P.cp[("statact", h)] = P.step(
                "ac", em, waits=[("dv", P.c(("statm", h)))], inc="ac",
                tag=f"statact{h}")

        def emit_apply(h):
            par = h % 2
            base = QK2 + par * 4096
            P.cp[("applyq", h)] = P.step(
                "dv", (lambda e, base=base: em_apply(e, base, 0, 64, 80)),
                waits=[("ac", P.c(("statact", h))), ("pl", P.c(("ropek", h)))],
                inc="dv", tag=f"applyq{h}")
            P.cp[("applyk", h)] = P.step(
                "dv", (lambda e, base=base: em_apply(e, base, 128, 96, 112)),
                inc="dv", tag=f"applyk{h}")

        AVC = [2048, 2304, 2560, 2816]

        def emit_attn(h):
            par = h % 2
            base = QK2 + par * 4096
            vxb = VX2 + par * 2112
            for p4 in range(8):
                qkoff = 0 if p4 < 4 else 128
                srcs = [bf(base + ((p4 % 4) * 4 + c) * 256 + qkoff,
                           base + ((p4 % 4) * 4 + c) * 256 + qkoff + 128)
                        for c in range(4)]
                dstb = (QT if p4 < 4 else KT) + (p4 % 4) * 512
                tpack(srcs, bf(dstb, dstb + 512),
                      [("dv", P.c(("applyq", h)) if p4 < 4 else P.c(("applyk", h)))],
                      f"qk{h}.{p4}")
            P.cp[("qktrl", h)] = P.c(("tp", gpk[0] - 1))
            cp_qktrev = P.c(("tpev", gpk[0] - 1))

            bidx = 0
            avb = {}

            def em_av(e, g, i):
                last = None
                c0 = (avb[(g, i)] % 3) * 512
                for j in range(max(i, 4 * g), 4 * g + 4):
                    cc = j - 4 * g
                    last = e.matmul(
                        pbig[:, AVC[cc]:AVC[cc] + 129],
                        bfar[:, ATG + c0 + cc * 128:ATG + c0 + (cc + 1) * 128],
                        bfar[:, vxb + i * 132:vxb + i * 132 + 129],
                        start=(i == 0 and cc % 2 == 0), stop=(i == j))
                return last

            def emit_av_for(g, i):
                bx = avb[(g, i)]
                w = [("dv", P.c(("sq2", h, bx)))]
                if i == 0:
                    pv = P.c(("aoevg", h, g - 1)) if g > 0 else P.c(("aoevg", h - 1, 3))
                    if pv:
                        w.append(("dv", pv))
                P.cp[("av", h, bx)] = P.step(
                    "te", (lambda e, g=g, i=i: em_av(e, g, i)), waits=w,
                    inc="pe", tag=f"av{h}.{g}.{i}")
                if i == 4 * g + 3:
                    def em_aoevg(e, g=g, vxb=vxb):
                        dn = _v(pbig[:, 2048:3072], [[256, 4], [1, 1]], doff=128)
                        rc = _v(stats[:, 240:244], [[1, 4], [1, 1]])
                        chain(e, "dv", e.reciprocal(rc, dn))
                        last = None
                        for cc in range(4):
                            j = 4 * g + cc
                            last = e.tensor_scalar(
                                bf(AOTM + j * 128, AOTM + (j + 1) * 128),
                                pbig[:, AVC[cc]:AVC[cc] + 128],
                                stats[:, 240 + cc:241 + cc], None, op0=AO.mult)
                        if g == 0:
                            chain(e, "dv", last)
                            last = e.tensor_copy(bfar[0:1, AOTM:AOTM + 128],
                                                 bfar[0:1, vxb:vxb + 128])
                        return last
                    P.cp[("aoevg", h, g)] = P.step(
                        "dv", em_aoevg, waits=[("pe", P.c(("av", h, bx)))],
                        inc="dv", tag=f"aoev{h}.{g}")

            for g in range(4):
                jmax = 4 * g + 3
                for i in range(jmax + 1):
                    w = []
                    if bidx == 0:
                        w.append(("dv", cp_qktrev))
                    if bidx < 2:
                        pv = P.c(("sq1", h - 1, 39))
                    else:
                        pv = P.c(("sq1", h, bidx - 2))
                    if pv:
                        w.append(("ac", pv))

                    def em_sc(e, g=g, i=i, bidx=bidx):
                        return e.matmul(
                            pbig[:, 1024 + (bidx % 2) * 512:1024 + (bidx % 2) * 512 + 512],
                            bfar[:, KT + i * 128:KT + (i + 1) * 128],
                            bfar[:, QT + g * 512:QT + (g + 1) * 512],
                            start=True, stop=True)
                    P.cp[("sc", h, bidx)] = P.step("te", em_sc, waits=w,
                                                   inc="pe", tag=f"sc{h}.{g}.{i}")
                    w = [("pe", P.c(("sc", h, bidx)))]
                    pv = P.c(("sq2", h, bidx - 3))
                    if pv:
                        w.append(("dv", pv))

                    def em_sq1(e, bidx=bidx):
                        return e.activation(
                            s2h[:, (bidx % 3) * 512:(bidx % 3) * 512 + 512],
                            pbig[:, 1024 + (bidx % 2) * 512:1024 + (bidx % 2) * 512 + 512],
                            AF.Square, scale=SQ_SCALE)
                    P.cp[("sq1", h, bidx)] = P.step("ac", em_sq1, waits=w,
                                                    inc="ac", tag=f"sq1.{h}.{bidx}")
                    diag = (4 * g <= i)

                    def em_sq2(e, bidx=bidx, diag=diag, cc=i - 4 * g):
                        sl = s2h[:, (bidx % 3) * 512:(bidx % 3) * 512 + 512]
                        if diag:
                            dsl = s2h[:, (bidx % 3) * 512 + cc * 128:
                                      (bidx % 3) * 512 + (cc + 1) * 128]
                            chain(e, "dv", e.tensor_tensor(dsl, dsl, tril16[:, :], op=AO.mult))
                        return e.tensor_tensor(
                            bfar[:, ATG + (bidx % 3) * 512:ATG + (bidx % 3 + 1) * 512],
                            sl, sl, op=AO.mult)
                    w = [("ac", P.c(("sq1", h, bidx)))]
                    pv = P.c(("av", h, bidx - 3))
                    if pv:
                        w.append(("pe", pv))
                    P.cp[("sq2", h, bidx)] = P.step("dv", em_sq2, waits=w,
                                                    inc="dv", tag=f"sq2.{h}.{bidx}")
                    avb[(g, i)] = bidx
                    if i >= 2:
                        emit_av_for(g, i - 2)
                    if i == jmax:
                        emit_av_for(g, i - 1)
                        emit_av_for(g, i)
                    bidx += 1
            P.cp[("scl", h)] = P.c(("sc", h, bidx - 1))
            P.cp[("avl", h)] = P.c(("av", h, bidx - 1))

            for p4 in range(4):
                srcs = [bf(AOTM + p4 * 512 + c * 128, AOTM + p4 * 512 + (c + 1) * 128)
                        for c in range(4)]
                tpack(srcs, aoT[:, h * 2048 + p4 * 512:h * 2048 + (p4 + 1) * 512],
                      [("dv", P.c(("aoevg", h, 3)))], f"ao{h}.{p4}")
            P.cp[("aotrl", h)] = P.c(("tp", gpk[0] - 1))
            P.cp[("aotrevl", h)] = P.c(("tpev", gpk[0] - 1))

        emit_proj(0)
        wq_dma_steps(1, [("pe", P.c(("projlast", 0)))])
        for h in range(H):
            emit_ew_start(h)
            hooks = {
                4: (lambda h=h: (emit_sq(h, "q"), emit_redq(h))),
                7: (lambda h=h: (emit_sq(h, "k"), emit_redk_stat(h))),
                11: (lambda h=h: (emit_statact(h), emit_apply(h))),
            }
            if h + 1 < H:
                emit_proj(h + 1, hooks)
            else:
                for fn in hooks.values():
                    fn()
            if h + 2 < H:
                wq_dma_steps(h + 2, [("pe", P.c(("projlast", h + 1)))])
            if h == H - 1:
                for c in range(NE):
                    P.step("gp", (lambda e, c=c: e.dma_start(
                        out=big[:, c * 2048:(c + 1) * 2048],
                        in_=woutd[c * 128:(c + 1) * 128, :])),
                        waits=([("pe", P.c(("projlast", H - 1)))] if c == 0 else ()),
                        inc="dwo", inc_amt=16, tag=f"wo{c}")
            emit_attn(h)

        P.step("gp", (lambda e: e.dma_start(
            out=bfar[:, UB:UB + 8192].rearrange("p (ec s) -> p ec s", ec=NE),
            in_=upermd.rearrange("(ec p) s -> p ec s", p=128))),
            waits=[("pe", P.c(("qktrl", H - 1)))],
            inc="duv", inc_amt=16, tag="udma")
        P.step("gp", (lambda e: e.dma_start(
            out=bfar[:, VB:VB + 8192].rearrange("p (sc e) -> p sc e", sc=4),
            in_=vflatd.rearrange("(sc p) e -> p sc e", p=128))),
            waits=[("pe", P.c(("avl", H - 1)))],
            inc="duv", inc_amt=16, tag="vdma")

        P.step("sy", (lambda e: e.dma_start(
            out=far[:, FB:FB + 2048], in_=xd[0:128, :])),
            waits=[("dv", P.c(("redk", H - 1)))], inc="dxr", inc_amt=16, tag="x0")
        P.step("sy", (lambda e: e.dma_start(
            out=far[:, FC:FC + 2048], in_=xd[128:256, :])),
            inc="dxr", inc_amt=16, tag="x1")

        def reg(t):
            b = FB if t % 2 == 0 else FC
            return far[:, b:b + 2048]

        h1bsc = bf(COSB, COSB + 2048)

        def emit_z(t):
            if t == 0:
                w = [("dwo", 256), ("dv", P.c(("aotrevl", H - 1)))]
            else:
                w = [("dv", P.c(("h1pre", t - 1)))]

            def em_z(e, t=t):
                last = None
                for ec in range(NE):
                    for nb in range(4):
                        last = e.matmul(
                            pbig[:, nb * 512:(nb + 1) * 512],
                            aoT[:, ec * 2048 + t * 128:ec * 2048 + t * 128 + 128],
                            big[:, ec * 2048 + nb * 512:ec * 2048 + (nb + 1) * 512],
                            start=(ec == 0), stop=(ec == NE - 1))
                return last
            P.cp[("z", t)] = P.step("te", em_z, waits=w, inc="pe", tag=f"z{t}")

        def emit_tile_chain(t):
            rg = reg(t)
            w = [("pe", P.c(("z", t))), ("dxr", 16 * (t + 1))]
            pv = P.c(("ln2sq", t - 2))
            if pv:
                w.append(("ac", pv))
            P.cp[("h1pre", t)] = P.step(
                "dv", (lambda e, rg=rg: e.tensor_tensor(rg, pbig[:, 0:2048], rg, op=AO.add)),
                waits=w, inc="dv", tag=f"h1pre{t}")
            def em_ln1sq(e, rg=rg):
                e.activation(h1bsc, rg, AF.Copy, accum_out=gcol(G_L1S))
                return e.activation(h1bsc, rg, AF.Square, accum_out=gcol(G_L1Q))
            P.cp[("ln1sq", t)] = P.step(
                "ac", em_ln1sq, waits=[("dv", P.c(("h1pre", t)))], inc="ac",
                tag=f"ln1sq{t}")

            def em_ln1m(e):
                e.tensor_scalar(gcol(G_NMU1), gcol(G_L1S), -1.0 / E, None, op0=AO.mult)
                chain(e, "dv", e.tensor_scalar(gcol(G_TMP), gcol(G_L1Q), 1.0 / E, None, op0=AO.mult))
                chain(e, "dv", e.tensor_tensor(gcol(G_V1), gcol(G_NMU1), gcol(G_NMU1), op=AO.mult))
                return e.tensor_tensor(gcol(G_V1), gcol(G_TMP), gcol(G_V1), op=AO.subtract)
            P.cp[("ln1m", t)] = P.step("dv", em_ln1m,
                                       waits=[("ac", P.c(("ln1sq", t)))],
                                       inc="dv", tag=f"ln1m{t}")

            def em_ln1a(e):
                chain(e, "ac", e.activation(gcol(G_TMP), gcol(G_V1), AF.Ln, bias=epsc[:, 0:1]))
                return e.activation(gcol(G_RS1), gcol(G_TMP), AF.Exp, scale=-0.5)
            P.cp[("ln1a", t)] = P.step("ac", em_ln1a,
                                       waits=[("dv", P.c(("ln1m", t)))],
                                       inc="ac", tag=f"ln1a{t}")

            def em_h1ap(e, rg=rg):
                e.tensor_scalar(bf(H1B, H1B + 2048), rg, gcol(G_NMU1), gcol(G_RS1),
                                op0=AO.add, op1=AO.mult)
                return e.tensor_scalar(rg, rg, gcol(G_NMU1), gcol(G_RS1),
                                       op0=AO.add, op1=AO.mult)
            P.cp[("h1", t)] = P.step("dv", em_h1ap,
                                     waits=[("ac", P.c(("ln1a", t)))],
                                     inc="dv", tag=f"h1ap{t}")

        def emit_h1tp(t):
            for p4 in range(4):
                srcs = [bf(H1B + p4 * 512 + c * 128, H1B + p4 * 512 + (c + 1) * 128)
                        for c in range(4)]
                tpack(srcs, bf(H1T + p4 * 512, H1T + (p4 + 1) * 512),
                      [("dv", P.c(("h1", t)))], f"h1{t}.{p4}")
                P.cp[("h1trev", t, p4)] = P.c(("tpev", gpk[0] - 1))
            P.cp[("h1trev", t)] = P.c(("tpev", gpk[0] - 1))

        def emit_schu(t):
            for half in range(2):
                w = [("dv", P.c(("h1trev", t, 2 * half + 1)))]
                if half == 0:
                    if t == 0:
                        w.append(("duv", 16))
                    else:
                        w += [("dv", P.c(("ffnev", t - 1, 3))),
                              ("ac", P.c(("e1", t - 1)))]

                def em_schu(e, half=half):
                    last = None
                    for ec in range(8 * half, 8 * half + 8):
                        e.matmul(pbig[:, 2048:2048 + S],
                                 bf(H1T + ec * 128, H1T + (ec + 1) * 128),
                                 wrtb[:, ec * S:(ec + 1) * S],
                                 start=(ec == 0), stop=(ec == NE - 1))
                        last = e.matmul(pbig[:, 2560:3072],
                                        bf(H1T + ec * 128, H1T + (ec + 1) * 128),
                                        bfar[:, UB + ec * 512:UB + (ec + 1) * 512],
                                        start=(ec == 0), stop=(ec == NE - 1))
                    return last
                P.cp[("schu", t)] = P.step("te", em_schu, waits=w, inc="pe",
                                           tag=f"schu{t}.{half}")

        def emit_gate(t):
            def em_gatea(e):
                psc = pbig[:, 2048:2048 + S]
                chain(e, "dv", e.reduce_max(gcol(G_M1), psc, axis=AX.X))
                e.tensor_scalar(gcol(G_NM1), gcol(G_M1), -1.0, None, op0=AO.mult)
                chain(e, "dv", e.tensor_copy(gat[:, G_SCF:G_SCF + S], psc))
                chain(e, "dv", e.tensor_scalar(gat[:, G_MK1:G_MK1 + S], gat[:, G_SCF:G_SCF + S],
                                gcol(G_M1), None, op0=AO.is_ge))
                chain(e, "dv", e.scalar_tensor_tensor(gat[:, G_SC2:G_SC2 + S],
                                       gat[:, G_MK1:G_MK1 + S], -1e30,
                                       gat[:, G_SCF:G_SCF + S],
                                       op0=AO.mult, op1=AO.add))
                chain(e, "dv", e.reduce_max(gcol(G_M2), gat[:, G_SC2:G_SC2 + S], axis=AX.X))
                return e.tensor_scalar(gat[:, G_MK:G_MK + S], gat[:, G_SCF:G_SCF + S],
                                       gcol(G_M2), None, op0=AO.is_ge)
            P.cp[("gatea", t)] = P.step("dv", em_gatea,
                                        waits=[("pe", P.c(("schu", t)))],
                                        inc="dv", tag=f"gatea{t}")
            P.cp[("e1", t)] = P.step(
                "ac", (lambda e: e.activation(gat[:, G_E1:G_E1 + S],
                                              pbig[:, 2048:2048 + S], AF.Exp,
                                              bias=gcol(G_NM1))),
                waits=[("dv", P.c(("gatea", t)))], inc="ac", tag=f"e1{t}")

            def em_gateb(e):
                e1v = gat[:, G_E1:G_E1 + S]
                e.tensor_reduce(gcol(G_Z), e1v, axis=AX.X, op=AO.add)
                chain(e, "dv", e.tensor_tensor(gat[:, G_G0:G_G0 + S], e1v, gat[:, G_MK:G_MK + S], op=AO.mult))
                chain(e, "dv", e.tensor_reduce(gcol(G_S0), gat[:, G_G0:G_G0 + S], axis=AX.X, op=AO.add))
                chain(e, "dv", e.scalar_tensor_tensor(gcol(G_DG), gcol(G_Z), 1e-8, gcol(G_S0),
                                       op0=AO.mult, op1=AO.add))
                chain(e, "dv", e.reciprocal(gcol(G_RG), gcol(G_DG)))
                chain(e, "dv", e.tensor_scalar(gat[:, G_GN:G_GN + S], gat[:, G_G0:G_G0 + S],
                                gcol(G_RG), None, op0=AO.mult))
                chain(e, "dv", e.tensor_tensor(gat[:, G_SCF:G_SCF + S], gat[:, G_GN:G_GN + S],
                                awb[:, :], op=AO.mult))
                chain(e, "dv", e.tensor_reduce(gcol(G_GAW), gat[:, G_SCF:G_SCF + S], axis=AX.X, op=AO.add))
                return e.tensor_scalar(gcol(G_GF), gcol(G_GAW), 0.2, 0.9,
                                       op0=AO.mult, op1=AO.add)
            P.cp[("gateb", t)] = P.step("dv", em_gateb,
                                        waits=[("ac", P.c(("e1", t)))],
                                        inc="dv", tag=f"gateb{t}")

            def em_wg(e):
                hu = _v(pbig[:, 2560:3072], [[16, S], [1, 16]])
                gn = _v(gat[:, G_GN:G_GN + S], [[1, S], [0, 16]])
                wo = _v(bf(WGB, WGB + 512), [[16, S], [1, 16]])
                return e.tensor_tensor(wo, hu, gn, op=AO.mult)
            P.cp[("wg", t)] = P.step("dv", em_wg, inc="dv", tag=f"wg{t}")

        def emit_wgtp(t):
            srcs = [bf(WGB + c * 128, WGB + (c + 1) * 128) for c in range(4)]
            tpack(srcs, bf(WGT, WGT + 512), [("dv", P.c(("wg", t)))], f"wg{t}")
            P.cp[("wgtrev", t)] = P.c(("tpev", gpk[0] - 1))

        def emit_ffn(t):
            rg = reg(t)
            for q in range(4):
                w = [("dv", P.c(("wgtrev", t)))]
                if q == 0:
                    w.append(("ac", P.c(("e1", t))))
                    if t == 0:
                        w.append(("duv", 32))
                elif q >= 2:
                    w.append(("dv", P.c(("ffnev", t, q - 2))))

                def em_q(e, q=q):
                    last = None
                    pcol = 2048 + (q % 2) * 512
                    for src in range(4):
                        last = e.matmul(
                            pbig[:, pcol:pcol + 512],
                            bf(WGT + src * 128, WGT + (src + 1) * 128),
                            bfar[:, VB + src * 2048 + q * 512:VB + src * 2048 + (q + 1) * 512],
                            start=(src == 0), stop=(src == 3))
                    return last
                P.cp[("ffnq", t, q)] = P.step("te", em_q, waits=w, inc="pe",
                                              tag=f"ffnq{t}.{q}")
                P.cp[("ffnev", t, q)] = P.step(
                    "dv", (lambda e, q=q, rg=rg: e.scalar_tensor_tensor(
                        rg[:, q * 512:(q + 1) * 512],
                        pbig[:, 2048 + (q % 2) * 512:2048 + (q % 2) * 512 + 512],
                        gcol(G_GF), rg[:, q * 512:(q + 1) * 512],
                        op0=AO.mult, op1=AO.add)),
                    waits=[("pe", P.c(("ffnq", t, q)))], inc="dv",
                    tag=f"ffnev{t}.{q}")

        def emit_ln2(t):
            rg = reg(t)
            def em_ln2sq(e, rg=rg):
                e.activation(h1bsc, rg, AF.Copy, accum_out=gcol(G_L2S))
                return e.activation(h1bsc, rg, AF.Square, accum_out=gcol(G_L2Q))
            P.cp[("ln2sq", t)] = P.step(
                "ac", em_ln2sq, waits=[("dv", P.c(("ffnev", t, 3)))], inc="ac",
                tag=f"ln2sq{t}")

            def em_ln2m(e):
                e.tensor_scalar(gcol(G_NMU2), gcol(G_L2S), -1.0 / E, None, op0=AO.mult)
                chain(e, "pl", e.tensor_scalar(gcol(G_TMP2), gcol(G_L2Q), 1.0 / E, None, op0=AO.mult))
                chain(e, "pl", e.tensor_tensor(gcol(G_V2), gcol(G_NMU2), gcol(G_NMU2), op=AO.mult))
                return e.tensor_tensor(gcol(G_V2), gcol(G_TMP2), gcol(G_V2), op=AO.subtract)
            P.cp[("ln2m", t)] = P.step("gp", em_ln2m,
                                       waits=[("ac", P.c(("ln2sq", t)))],
                                       inc="pl", tag=f"ln2m{t}")

            def em_ln2a(e):
                chain(e, "ac", e.activation(gcol(G_TMP2), gcol(G_V2), AF.Ln, bias=epsc[:, 0:1]))
                return e.activation(gcol(G_RS2), gcol(G_TMP2), AF.Exp, scale=-0.5)
            P.cp[("ln2a", t)] = P.step("ac", em_ln2a,
                                       waits=[("pl", P.c(("ln2m", t)))],
                                       inc="ac", tag=f"ln2a{t}")
            P.cp[("y", t)] = P.step(
                "gp", (lambda e, rg=rg: e.tensor_scalar(
                    rg, rg, gcol(G_NMU2), gcol(G_RS2), op0=AO.add, op1=AO.mult)),
                waits=[("ac", P.c(("ln2a", t)))], inc="pl", tag=f"yap{t}")
            P.step("sy", (lambda e, t=t, rg=rg: e.dma_start(
                out=yd[t * 128:(t + 1) * 128, :], in_=rg)),
                waits=[("pl", P.c(("y", t)))], inc="dy", inc_amt=16, tag=f"ydma{t}")
            if t + 2 < NT:
                P.step("sy", (lambda e, t=t: e.dma_start(
                    out=reg(t), in_=xd[(t + 2) * 128:(t + 3) * 128, :])),
                    waits=[("dy", 16 * (t + 1))], inc="dxr", inc_amt=16,
                    tag=f"x{t + 2}")

        emit_z(0)
        emit_tile_chain(0)
        for t in range(NT):
            emit_h1tp(t)
            emit_schu(t)
            if t + 1 < NT:
                emit_z(t + 1)
            emit_gate(t)
            emit_wgtp(t)
            if t + 1 < NT:
                emit_tile_chain(t + 1)
            emit_ffn(t)
            emit_ln2(t)

        P.step("sy", (lambda e: e.nop()), waits=[("dy", 16 * NT)], tag="fin")

        steps = P.steps
        if cut is not None:
            idx = next(i for i, st in enumerate(steps) if st["tag"] == cut)
            steps = steps[:idx + 1]
            cnts = {}
            for st in steps:
                if st["inc"]:
                    cnts[st["inc"]] = cnts.get(st["inc"], 0) + st["inc_amt"]
            barrier = [(n, v) for n, v in cnts.items()
                       if n in ("pe", "dv", "ac", "pl")]
            ndma = 0
            env = dict(bfar=bfar, far=far, big=big, aoT=aoT, s2h=s2h, pbig=pbig,
                       stats=stats, gat=gat, awb=awb)
            for name, apfn, shape, dt in taps:
                ap = apfn(env)
                steps.append(dict(engine="sy",
                                  emit=(lambda e, o=tap_outs[name], i=ap:
                                        e.dma_start(out=o, in_=i)),
                                  waits=barrier, inc="dtap", inc_amt=16,
                                  tag="tap" + name))
                ndma += 1
            steps.append(dict(engine="sy", emit=(lambda e: e.nop()),
                              waits=[("dtap", 16 * ndma)], inc=None, inc_amt=1,
                              tag="tapfin"))

        def walker(key):
            def run(eng):
                for st in steps:
                    if st["engine"] != key:
                        continue
                    for s, v in st["waits"]:
                        eng.wait_ge(sems[s], v)
                    last = st["emit"](eng)
                    if st["inc"] is not None:
                        assert last is not None, st["tag"]
                        last.then_inc(sems[st["inc"]], st["inc_amt"])
            return run

        block.tensor(walker("te"))
        block.vector(walker("dv"))
        block.scalar(walker("ac"))
        block.gpsimd(walker("gp"))
        block.sync(walker("sy"))
    return nc


def _host_inputs(L0_attn_x, L0_attn_w_qkv, L0_attn_w_out, L0_attn_ln_scale,
                 L0_attn_ln_bias, L0_ffn_Wr, L0_ffn_U, L0_ffn_V, L0_ffn_attr,
                 L0_ffn_Wa):
    x = np.asarray(L0_attn_x, dtype=np.float32)
    wqkv = np.ascontiguousarray(np.asarray(L0_attn_w_qkv, np.float32))
    wout = np.ascontiguousarray(np.asarray(L0_attn_w_out, np.float32))
    uperm = np.ascontiguousarray(
        np.asarray(L0_ffn_U, np.float32).transpose(1, 0, 2).reshape(E, S * R))
    vflat = np.ascontiguousarray(np.asarray(L0_ffn_V, np.float32).reshape(S * R, E))
    wrt = np.ascontiguousarray(np.asarray(L0_ffn_Wr, np.float32).T)
    attrT = np.ascontiguousarray(np.asarray(L0_ffn_attr, np.float32).T)
    waT = np.ascontiguousarray(np.asarray(L0_ffn_Wa, np.float32).reshape(S, 1))
    inv = (1.0 / (10000.0 ** (np.arange(0, D, 2, dtype=np.float32) /
                              np.float32(D)))).astype(np.float32)
    tt = np.arange(T, dtype=np.float32)
    ang = tt[:, None] * inv[None, :]
    cos = np.cos(ang).astype(np.float32)
    sin = np.sin(ang).astype(np.float32)
    cosr = np.ascontiguousarray(cos.reshape(NT, 128, 64).transpose(1, 0, 2)
                                .reshape(128, NT * 64))
    sinr = np.ascontiguousarray(sin.reshape(NT, 128, 64).transpose(1, 0, 2)
                                .reshape(128, NT * 64))
    scb = np.ascontiguousarray(np.broadcast_to(
        np.asarray(L0_attn_ln_scale, np.float32)[None, :], (128, D)))
    bib = np.ascontiguousarray(np.broadcast_to(
        np.asarray(L0_attn_ln_bias, np.float32)[None, :], (128, D)))
    trilm = np.ascontiguousarray(np.triu(np.ones((128, 128), np.float32)))
    ident = np.eye(128, dtype=ml_dtypes.bfloat16)
    shared = dict(wqkv=wqkv, wout=wout, uperm=uperm, vflat=vflat, wrt=wrt,
                  attrT=attrT, waT=waT, cosr=cosr, sinr=sinr, scb=scb, bib=bib,
                  trilm=trilm, ident=ident)
    in_maps = []
    for b in range(8):
        m = dict(shared)
        m["x"] = np.ascontiguousarray(x[b])
        m["xT"] = np.ascontiguousarray(x[b].T)
        in_maps.append(m)
    return in_maps


def kernel(**inputs):
    in_maps = _host_inputs(**inputs)
    nc = build()
    res = run_bass_kernel_spmd(nc, in_maps, core_ids=list(range(8)))
    return np.stack([np.asarray(r["y"], dtype=np.float32) for r in res.results],
                    axis=0)
